# version 22
# baseline (speedup 1.0000x reference)
import numpy as np
from contextlib import ExitStack

import concourse.bass as bass
import concourse.mybir as mybir
from concourse.bass_utils import run_bass_kernel_spmd

F32 = mybir.dt.float32
BF16 = mybir.dt.bfloat16
AF = mybir.ActivationFunctionType
OP = mybir.AluOpType
AX = mybir.AxisListType

T = 2048
D = 1024
C = 1032
SD = 8
I = 2816
NTH = 8
CG = 704
NCG = I // CG
KD = D // 128
KI = I // 128
EPS = 1e-5


class Prog:

    def __init__(self):
        self.ops = []
        self.last_writer = {}
        self.readers = {}

    def add(self, stream, fn, reads=(), writes=(), dma=False, tag="",
            big=False, sem=None):
        idx = len(self.ops)
        deps = set()
        for k in reads:
            j = self.last_writer.get(k)
            if j is not None:
                deps.add(j)
        for k in writes:
            j = self.last_writer.get(k)
            if j is not None:
                deps.add(j)
            deps.update(self.readers.get(k, ()))
        deps.discard(idx)
        self.ops.append(dict(stream=stream, fn=fn,
                             sem=sem or (stream + ("q" if dma else "")),
                             step=16 if dma else 1, deps=deps, dma=dma,
                             tag=tag, need=False, rank=None, waits=None,
                             big=big, drain=False))
        for k in reads:
            self.readers.setdefault(k, []).append(idx)
        for k in writes:
            self.last_writer[k] = idx
            self.readers[k] = []
        return idx

    def plan(self, force_need=()):
        ops = self.ops
        for op in ops:
            for j in op["deps"]:
                dj = ops[j]
                if dj["stream"] != op["stream"] or dj["dma"]:
                    dj["need"] = True
                if (not dj["dma"] and dj["stream"] not in ("pe", "sp")
                        and (not dj["big"]
                             or dj["stream"] in ("dve", "pool"))):
                    if dj["stream"] in ("dve", "pool"):
                        dj["need"] = True
                    else:
                        dj["drain"] = True
        for i in force_need:
            ops[i]["need"] = True
        forced = set(force_need)
        for i, op in enumerate(ops):
            if op["dma"]:
                op["need"] = True
                forced.add(i)
        sem_ops = {}
        for i, op in enumerate(ops):
            sem_ops.setdefault(op["sem"], []).append(i)
        import bisect
        for _ in range(6):
            counters = {}
            for op in ops:
                op["rank"] = None
            for sem, lst in sem_ops.items():
                r = 0
                for i in lst:
                    if ops[i]["need"]:
                        r += 1
                        ops[i]["rank"] = r
                counters[sem] = r
            know_stream = {}
            know_op = {}
            wait_targets = set()
            for i, op in enumerate(ops):
                s = op["stream"]
                know = know_stream.setdefault(s, {})
                req = {}
                for j in sorted(op["deps"]):
                    dj = ops[j]
                    if (dj["stream"] == s and not dj["dma"]
                            and s not in ("pool", "dve")):
                        continue
                    tj = j
                    if not dj["need"]:
                        lst = sem_ops[dj["sem"]]
                        p = bisect.bisect_left(lst, j)
                        tj = None
                        while p < len(lst):
                            if ops[lst[p]]["need"]:
                                tj = lst[p]
                                break
                            p += 1
                        assert tj is not None, (i, j, dj["tag"])
                    dt = ops[tj]
                    cur = req.get(dt["sem"])
                    if cur is None or dt["rank"] > cur[0]:
                        req[dt["sem"]] = (dt["rank"], tj)
                waits = []
                for sem, (rank, tj) in req.items():
                    if know.get(sem, 0) >= rank:
                        continue
                    dt = ops[tj]
                    waits.append((sem, rank * dt["step"]))
                    wait_targets.add(tj)
                    know[sem] = rank
                    for sem2, r2 in know_op.get(tj, {}).items():
                        if know.get(sem2, 0) < r2:
                            know[sem2] = r2
                op["waits"] = waits
                snap = dict(know)
                if op["need"]:
                    snap[op["sem"]] = max(snap.get(op["sem"], 0), op["rank"])
                know_op[i] = snap
            new_need = wait_targets | forced
            old_need = {i for i, op in enumerate(ops) if op["need"]}
            if new_need == old_need:
                break
            for i, op in enumerate(ops):
                op["need"] = i in new_need
        self.sem_totals = counters

    def emit_stream(self, eng, stream, sems):
        for op in self.ops:
            if op["stream"] != stream:
                continue
            for sem_name, val in op["waits"]:
                eng.wait_ge(sems[sem_name], val)
            instr = op["fn"](eng)
            if op["drain"]:
                instr = eng.drain()
            if op["need"]:
                instr.then_inc(sems[op["sem"]], op["step"])


def build_nc(dbg=False):
    nc = bass.Bass("TRN2", target_bir_lowering=False)

    x_d = nc.dram_tensor("x", [T, D], F32, kind="ExternalInput").ap()
    w_in_d = nc.dram_tensor("w_in", [D, C], F32, kind="ExternalInput").ap()
    b_in_d = nc.dram_tensor("b_in", [1, C], F32, kind="ExternalInput").ap()
    w_g_d = nc.dram_tensor("w_g", [C, SD], F32, kind="ExternalInput").ap()
    b_g_d = nc.dram_tensor("b_g", [1, SD], F32, kind="ExternalInput").ap()
    w_out_d = nc.dram_tensor("w_out", [C, D], F32, kind="ExternalInput").ap()
    b_out_d = nc.dram_tensor("b_out", [1, D], F32, kind="ExternalInput").ap()
    w_gate_d = nc.dram_tensor("w_gate", [I, D], F32, kind="ExternalInput").ap()
    w_up_d = nc.dram_tensor("w_up", [I, D], F32, kind="ExternalInput").ap()
    w_down_d = nc.dram_tensor("w_down", [D, I], F32, kind="ExternalInput").ap()
    out_d = nc.dram_tensor("out", [T, D], F32, kind="ExternalOutput").ap()
    wqT_g_d = nc.dram_tensor("wqT_g", [D, I], BF16, kind="Internal").ap()
    wqT_u_d = nc.dram_tensor("wqT_u", [D, I], BF16, kind="Internal").ap()
    wqT_d_d = nc.dram_tensor("wqT_d", [I, D], BF16, kind="Internal").ap()
    dbg_d = {}
    if dbg:
        for nm, shp, dt in [("dbg_uq", [128, D], BF16),
                            ("dbg_wg", [D, I], BF16),
                            ("dbg_wd", [I, D], BF16),
                            ("dbg_hq", [128, I], BF16),
                            ("dbg_r1", [T, D], F32),
                            ("dbg_r2", [T, D], F32),
                            ("dbg_cmb", [128, 9 * 128], F32),
                            ("dbg_st", [SD, T], F32)]:
            dbg_d[nm] = nc.dram_tensor(nm, shp, dt, kind="ExternalOutput").ap()

    es = ExitStack()
    sb = lambda name, shape, dt: es.enter_context(nc.sbuf_tensor(name, shape, dt))
    ps = lambda name, shape, dt: es.enter_context(nc.psum_tensor(name, shape, dt))

    NQ = 4
    NGRP = (T // 128) // NQ
    QW = 1024
    CGS = []
    o = 0
    while o < I:
        w = min(512, I - o)
        CGS.append((o, w))
        o += w

    r_sb = sb("r", [128, NQ, D], F32)
    s1 = sb("s1", [128, KI, D], BF16)
    s2 = sb("s2", [128, NQ * I], BF16)
    NPAR = 2
    b1 = sb("b1", [128, NPAR, QW], F32)
    b2 = sb("b2", [128, NPAR, QW], F32)
    b3 = sb("b3", [128, NPAR, QW], BF16)
    b4 = sb("b4", [128, NPAR, QW], BF16)
    b5 = sb("b5", [128, NPAR, QW], BF16)
    hst = sb("hst", [128, 4, QW], F32)
    wstp = hst[:, 0:2, :]
    s4 = sb("s4", [128, 3, KD, 512], BF16)
    uqT = sb("uqT", [128, NQ, KD, 128], BF16)
    hqT = sb("hqT", [128, 2, KI, 128], BF16)
    u_b = sb("u", [128, D], F32)
    v_b = sb("v", [128, 2, D], BF16)
    wst = sb("wst", [128, C], F32)
    big16 = sb("big16", [128, 8192], BF16)
    wrow = big16[:, 0:I]
    ident = sb("ident", [128, 128], BF16)
    ones_b = sb("ones", [1, 128], BF16)
    bin_b = sb("bin_b", [1, C], BF16)
    bg_b = sb("bg_b", [1, SD], BF16)
    bout_b = sb("bout_b", [1, D], BF16)
    wg_b = sb("wg_b", [128, KD, SD], BF16)
    wg8_b = sb("wg8_b", [SD, SD], BF16)
    w8o_b = sb("w8o_b", [SD, D], BF16)
    sums = sb("sums", [128, NQ], F32)
    sqs = sb("sqs", [128, NQ], F32)
    mcol = sb("mcol", [128, NQ], F32)
    varc = sb("varc", [128, NQ], F32)
    stdc = sb("stdc", [128, NQ], F32)
    rstd = sb("rstd", [128, NQ], F32)
    msq = sb("msq", [128, NQ], F32)
    NG = I // 16
    gmax = sb("gmax", [128, NPAR, QW // 16], F32)
    scal = sb("scal", [128, NPAR, QW // 16], F32)
    rinv = sb("rinv", [128, NPAR, QW // 16], F32)
    g_sb = sb("g_sb", [SD, 128], F32)
    si_sb = sb("si_sb", [SD, 128], F32)
    states = sb("states", [SD, 128], F32)
    states_b = sb("states_b", [SD, 128], BF16)
    carry = sb("carry", [SD, 1], F32)
    cmb8_b = sb("cmb8_b", [SD, 128], BF16)
    eps_c = sb("eps_c", [128, 1], F32)
    nb2 = sb("nb2", [128, 1], F32)

    AW_CMB = KD * C
    s2f = s2[:, :]
    win_b = s2f[:, 0:KD * C].rearrange("p (k c) -> p k c", c=C)
    wout_b = big16[:, :].rearrange("p (k c) -> p k c", c=D)
    cmbT = s2f[:, AW_CMB:AW_CMB + 8 * 128].rearrange("p (j c) -> p j c", c=128)
    hq_v = s2f.rearrange("p (t i) -> p t i", i=I)
    wq_v = s2f.rearrange("p (r w) -> p r w", w=D)
    wqd_v = s2f.rearrange("p (r w) -> p r w", w=I)

    def s2k(lo, hi):
        return [("s2", p) for p in range(lo // I, (hi - 1) // I + 1)]

    pG = [ps("pG0", [128, 512], F32), ps("pG1", [128, 512], F32)]
    pU = [ps("pU0", [128, 512], F32), ps("pU1", [128, 512], F32)]
    pF = ps("pF", [128, D], F32)
    pT = ps("pT", [128, 1024], BF16)
    pT2 = ps("pT2", [128, 1024], BF16)
    kG = [("pG0",), ("pG1",)]
    kU = [("pU0",), ("pU1",)]

    P = Prog()
    BIG_TAGS = {"qnorm", "qsgn", "qsm", "qout", "qk1", "qk2", "qk3",
                "qc1", "qt2", "qu1", "qt3", "qq2", "lnsq",
                "lnapply", "silu", "hmul", "resid", "aresid", "utrc", "htrc",
                "vtrc", "wtrc", "wtos1", "awcast", "smcast"}
    dve = lambda fn, reads=(), writes=(), tag="": P.add(
        "dve", fn, reads, writes, tag=tag, big=tag in BIG_TAGS)
    act = lambda fn, reads=(), writes=(), tag="": P.add(
        "act", fn, reads, writes, tag=tag, big=tag in BIG_TAGS)
    pool = lambda fn, reads=(), writes=(), tag="": P.add(
        "pool", fn, reads, writes, tag=tag, big=tag in BIG_TAGS)
    pe = lambda fn, reads=(), writes=(), tag="": P.add("pe", fn, reads, writes, tag=tag)
    sp_dma = lambda fn, reads=(), writes=(), tag="", sem=None: P.add(
        "sp", fn, reads, writes, dma=True, tag=tag, sem=sem)
    pool_dma = lambda fn, reads=(), writes=(), tag="", sem=None: P.add(
        "pool", fn, reads, writes, dma=True, tag=tag, sem=sem)

    out_store_ids = []

    pool(lambda e: e.memset(ident[:, :], 0.0), writes=[("ident",)])
    pool(lambda e: e.affine_select(
        out=ident[:, :], in_=ident[:, :], compare_op=OP.not_equal,
        fill=1.0, base=0, pattern=[[-1, 128]], channel_multiplier=1),
        reads=[("ident",)], writes=[("ident",)])
    pool(lambda e: e.memset(ones_b[:, :], 1.0), writes=[("ones",)])
    pool(lambda e: e.memset(carry[:, :], 0.0), writes=[("carry",)])
    pool(lambda e: e.memset(eps_c[:, :], EPS), writes=[("eps_c",)])
    pool(lambda e: e.memset(nb2[:, :], -2.0), writes=[("nb2",)])

    def small_load(dst_ap, dst_key, src, rows, cols):
        sp_dma(lambda e: e.dma_start(out=wst[0:rows, 0:cols], in_=src),
               reads=[("wst",)], writes=[("wst",)], tag="smload", sem="smq")
        dve(lambda e: e.tensor_copy(dst_ap, wst[0:rows, 0:cols]),
            reads=[("wst",)], writes=[dst_key], tag="smcast")

    small_load(bin_b[:, :], ("bin_b",), b_in_d[:, :], 1, C)
    small_load(bg_b[:, :], ("bg_b",), b_g_d[:, :], 1, SD)
    small_load(bout_b[:, :], ("bout_b",), b_out_d[:, :], 1, D)
    sp_dma(lambda e: e.dma_start(
        out=wst[:, 0:KD * SD].rearrange("p (k s) -> p k s", s=SD),
        in_=w_g_d[0:D, :].rearrange("(k p) s -> p k s", p=128)),
        reads=[("wst",)], writes=[("wst",)], tag="smload", sem="smq")
    dve(lambda e: e.tensor_copy(
        wg_b[:, :, :], wst[:, 0:KD * SD].rearrange("p (k s) -> p k s", s=SD)),
        reads=[("wst",)], writes=[("wg_b",)], tag="smcast")
    small_load(wg8_b[:, :], ("wg8_b",), w_g_d[D:C, :], SD, SD)
    small_load(w8o_b[:, :], ("w8o_b",), w_out_d[D:C, :], SD, D)

    I32 = mybir.dt.int32
    b1i = b1[:, :, :].bitcast(I32)
    b2i = b2[:, :, :].bitcast(I32)
    b2h = b2[:, :, :].bitcast(BF16)

    def quant_ops(src_ap, src_keys, width, out_ap, out_keys, goff, par):
        assert width <= QW and width % 16 == 0
        G = width // 16
        gm = gmax[:, par, 0:G]
        sc = scal[:, par, 0:G]
        ri = rinv[:, par, 0:G]
        src_g = src_ap.rearrange("p (g k) -> p g k", k=16)
        vv = b1[:, par, 0:width]
        kk = b2i[:, par, 0:width]
        sg = b2h[:, par, 0:width]
        c1 = b3[:, par, 0:width]
        c2 = b4[:, par, 0:width]
        c3 = b5[:, par, 0:width]
        w2 = b1[:, par, 0:width]
        v_g = vv.rearrange("p (g k) -> p g k", k=16)
        w2g = w2.rearrange("p (g k) -> p g k", k=16)
        sg_g = sg.rearrange("p (g k) -> p g k", k=16)
        rib = ri.unsqueeze(2).broadcast_to([128, G, 16])
        scb = sc.unsqueeze(2).broadcast_to([128, G, 16])
        kq1, kq2 = ("qb1", par), ("qb2", par)
        kc1, kc2, kc3 = ("qb3", par), ("qb4", par), ("qb5", par)
        kgm, ksc, kri = ("gmax", par), ("scal", par), ("rinv", par)
        src_keys = list(src_keys)
        out_keys = list(out_keys)
        return [
            (dve, lambda e: e.tensor_reduce(gm, src_g, axis=AX.X, op=OP.max,
                                            apply_absolute_value=True),
             src_keys, [kgm], "qred"),
            (dve, lambda e: e.tensor_scalar(sc, gm, 1.0 / 6.0, 1e-12,
                                            op0=OP.mult, op1=OP.max),
             [kgm], [ksc], "qsc"),
            (dve, lambda e: e.reciprocal(ri, sc), [ksc], [kri], "qri"),
            (dve, lambda e: e.tensor_tensor(v_g, src_g, rib, op=OP.mult),
             src_keys + [kri], [kq1], "qnorm"),
            (act, lambda e: e.activation(out=kk, in_=vv, func=AF.Abs,
                                         scale=2.0),
             [kq1], [kq2], "qk1"),
            (dve, lambda e: e.tensor_scalar(c1, kk, 4.0, 0.5,
                                            op0=OP.min, op1=OP.mult),
             [kq2], [kc1], "qc1"),
            (act, lambda e: e.activation(out=kk, in_=vv, func=AF.Abs,
                                         scale=1.0),
             [kq1, kq2], [kq2], "qk2"),
            (dve, lambda e: e.tensor_scalar(c2, kk, 2.0, 0.0,
                                            op0=OP.subtract, op1=OP.max),
             [kq2], [kc2], "qt2"),
            (dve, lambda e: e.scalar_tensor_tensor(c2, c2, 2.0, c1,
                                                   op0=OP.min, op1=OP.add),
             [kc2, kc1], [kc2], "qu1"),
            (act, lambda e: e.activation(out=kk, in_=vv, func=AF.Abs,
                                         scale=0.5),
             [kq1, kq2], [kq2], "qk3"),
            (dve, lambda e: e.tensor_scalar(c3, kk, 2.0, 0.0,
                                            op0=OP.subtract, op1=OP.max),
             [kq2], [kc3], "qt3"),
            (dve, lambda e: e.scalar_tensor_tensor(c1, c3, 2.0, c2,
                                                   op0=OP.mult, op1=OP.add),
             [kc3, kc2, kc1], [kc1], "qq2"),
            (act, lambda e: e.activation(out=sg, in_=vv, func=AF.Sign),
             [kq1, kq2], [kq2], "qsgn"),
            (dve, lambda e: e.tensor_tensor(w2g, sg_g, scb, op=OP.mult),
             [kq2, ksc, kq1], [kq1], "qsm"),
            (pool, lambda e: e.tensor_tensor(out_ap, c1, w2, op=OP.mult),
             [kc1, kq1], out_keys, "qout"),
        ]

    def quant_wave(call_list):
        if not call_list:
            return
        nst = len(call_list[0])
        for s in range(nst):
            for ops in call_list:
                adder, fn, reads, writes, tag = ops[s]
                adder(fn, reads=reads, writes=writes, tag=tag)

    def quant(src_ap, src_keys, width, out_ap, out_keys, goff, par):
        quant_wave([quant_ops(src_ap, src_keys, width, out_ap, out_keys,
                              goff, par)])

    def wq_weight2(w_d, nrt, wcols, to_hbm_ap, hbm_key, wsem):
        stg_v = wq_v if wcols == D else wqd_v
        rows_fit = (NQ * I) // wcols
        pq = 0
        for rbase in range(0, nrt, rows_fit):
            rtop = min(rbase + rows_fit, nrt)
            subs = []
            for r in range(rbase, rtop):
                off = 0
                while off < wcols:
                    sw = min(QW, wcols - off)
                    subs.append((r, off, sw))
                    off += sw
            for wv in range(0, len(subs), 2):
                wave = subs[wv:wv + 2]
                calls = []
                for wi2, (r, off, sw) in enumerate(wave):
                    wp = wi2
                    sp_dma(lambda e, r=r, off=off, sw=sw, wp=wp: e.dma_start(
                        out=wstp[:, wp, 0:sw],
                        in_=w_d[r * 128:(r + 1) * 128, off:off + sw]),
                        reads=[("hst", wp)], writes=[("hst", wp)],
                        tag="wload", sem="wlq%d" % wp)
                    sr = r - rbase
                    lo = sr * wcols + off
                    calls.append(quant_ops(
                        wstp[:, wp, 0:sw], [("hst", wp)], sw,
                        stg_v[:, sr, off:off + sw], s2k(lo, lo + sw),
                        off // 16, wp))
                quant_wave(calls)
                pq += len(wave)
            nkc = wcols // 128
            for k in range(nkc):
                for r in range(rbase, rtop):
                    sr = r - rbase
                    grp = (sr // 8) % 2
                    pt = pT if grp == 0 else pT2
                    pk = ("pT",) if grp == 0 else ("pT2",)
                    lo = sr * wcols + k * 128
                    pe(lambda e, k=k, sr=sr, pt=pt: e.transpose(
                        pt[:, (sr % 8) * 128:(sr % 8 + 1) * 128],
                        stg_v[:, sr, k * 128:(k + 1) * 128], ident[:, :]),
                       reads=s2k(lo, lo + 128) + [("ident",)], writes=[pk],
                       tag="wtr")
                    if sr % 8 == 7 or r == rtop - 1:
                        lo8 = (sr // 8) * 8
                        n = sr - lo8 + 1
                        act(lambda e, lo8=lo8, n=n, pt=pt, rbase=rbase: e.copy(
                            wrow[:, (rbase + lo8) * 128:(rbase + lo8 + n) * 128],
                            pt[:, 0:n * 128]),
                            reads=[pk, ("wrow",)], writes=[("wrow",)], tag="wtrc")
                sp_dma(lambda e, k=k, rbase=rbase, rtop=rtop: e.dma_start(
                    out=to_hbm_ap[k * 128:(k + 1) * 128,
                                  rbase * 128:rtop * 128],
                    in_=wrow[:, rbase * 128:rtop * 128]),
                    reads=[("wrow",)], writes=[hbm_key], tag="wstore",
                    sem=wsem)

    wq_weight2(w_gate_d, KI, D, wqT_g_d, ("hbm_wg",), "wsg")
    wq_weight2(w_up_d, KI, D, wqT_u_d, ("hbm_wu",), "wsu")
    wq_weight2(w_down_d, KD, I, wqT_d_d, ("hbm_wd",), "wsd")

    if dbg:
        out_store_ids.append(pool_dma(
            lambda e: e.dma_start(out=dbg_d["dbg_wg"][:, :], in_=wqT_g_d[:, :]),
            reads=[("hbm_wg",)], tag="dbgwg"))
        out_store_ids.append(pool_dma(
            lambda e: e.dma_start(out=dbg_d["dbg_wd"][:, :], in_=wqT_d_d[:, :]),
            reads=[("hbm_wd",)], tag="dbgwd"))

    for kk in range(0, KI, 6):
        kn = min(6, KI - kk)
        sp_dma(lambda e, kk=kk, kn=kn: e.dma_start(
            out=s1[:, kk:kk + kn, :],
            in_=wqT_d_d[kk * 128:(kk + kn) * 128, :].rearrange(
                "(k p) n -> p k n", p=128)),
            reads=[("hbm_wd",), ("s1",)], writes=[("s1",)], tag="s1load",
            sem="slq")

    def ln_stats():
        for t in range(NQ):
            dve(lambda e, t=t: e.tensor_reduce(
                sums[:, t:t + 1], r_sb[:, t, :], axis=AX.X, op=OP.add),
                reads=[("r", t)], writes=[("sums",)], tag="lnsum")
            act(lambda e, t=t: e.activation(
                out=u_b[:, :], in_=r_sb[:, t, :], func=AF.Square),
                reads=[("r", t), ("u",)], writes=[("u",)], tag="lnsq")
            dve(lambda e, t=t: e.tensor_reduce(
                sqs[:, t:t + 1], u_b[:, :], axis=AX.X, op=OP.add),
                reads=[("u",)], writes=[("sqs",)], tag="lnsq2")
        dve(lambda e: e.tensor_scalar(mcol[:, :], sums[:, :], 1.0 / D, 0.0,
                                      op0=OP.mult, op1=OP.add),
            reads=[("sums",)], writes=[("mcol",)])
        dve(lambda e: e.tensor_scalar(varc[:, :], sqs[:, :], 1.0 / D, 0.0,
                                      op0=OP.mult, op1=OP.add),
            reads=[("sqs",)], writes=[("varc",)])
        dve(lambda e: e.tensor_tensor(msq[:, :], mcol[:, :], mcol[:, :],
                                      op=OP.mult),
            reads=[("mcol",)], writes=[("msq",)])
        dve(lambda e: e.tensor_tensor(varc[:, :], varc[:, :], msq[:, :],
                                      op=OP.subtract),
            reads=[("varc",), ("msq",)], writes=[("varc",)])
        act(lambda e: e.activation(out=stdc[:, :], in_=varc[:, :],
                                   func=AF.Sqrt, bias=eps_c[:, :], scale=1.0),
            reads=[("varc",), ("eps_c",)], writes=[("stdc",)], tag="lnsqrt")
        dve(lambda e: e.reciprocal(rstd[:, :], stdc[:, :]),
            reads=[("stdc",)], writes=[("rstd",)])

    def ffn_call(grp, is_last):
        ln_stats()
        for t in range(NQ):
            dve(lambda e, t=t: e.tensor_scalar(
                u_b[:, :], r_sb[:, t, :], mcol[:, t:t + 1], rstd[:, t:t + 1],
                op0=OP.subtract, op1=OP.mult),
                reads=[("r", t), ("mcol",), ("rstd",), ("u",)], writes=[("u",)],
                tag="lnapply")
            quant_wave([quant_ops(
                u_b[:, :], [("u",)], D,
                v_b[:, t % 2, :], [("v", t % 2)], 0, t % 2)])
            if dbg and grp == 0 and t == 0 and not is_last and dbg_d.get("dbg_uq") is not None:
                _a = dbg_d["dbg_uq"]
                out_store_ids.append(pool_dma(
                    lambda e, _a=_a, t=t: e.dma_start(out=_a[:, :], in_=v_b[:, t % 2, :]),
                    reads=[("v", t % 2)], tag="dbguq"))
                dbg_d["dbg_uq"] = None
            for k in range(KD):
                pe(lambda e, k=k, t=t: e.transpose(
                    pT[:, k * 128:(k + 1) * 128],
                    v_b[:, t % 2, k * 128:(k + 1) * 128], ident[:, :]),
                   reads=[("v", t % 2), ("ident",)], writes=[("pT",)], tag="utr")
            act(lambda e, t=t: e.copy(
                uqT[:, t, :, :].rearrange("p a b -> p (a b)"), pT[:, :]),
                reads=[("pT",), ("uqT", t)], writes=[("uqT", t)], tag="utrc")
        for cgi, (cgo, cgw) in enumerate(CGS):
            slot_g = (2 * cgi) % 3
            slot_u = (2 * cgi + 1) % 3
            for slot, wsrc, hk in ((slot_g, wqT_g_d, ("hbm_wg",)),
                                   (slot_u, wqT_u_d, ("hbm_wu",))):
                sp_dma(lambda e, slot=slot, wsrc=wsrc, cgo=cgo, cgw=cgw:
                       e.dma_start(
                           out=s4[:, slot, :, 0:cgw],
                           in_=wsrc[:, cgo:cgo + cgw].rearrange(
                               "(k p) n -> p k n", p=128)),
                       reads=[("s4", slot)], writes=[("s4", slot)],
                       tag="wchunk", sem="wcq%d" % slot)
            calls = []
            for t in range(NQ):
                tp = t % 2
                for slot, pd, pk in ((slot_g, pG[tp], kG[tp]),
                                     (slot_u, pU[tp], kU[tp])):
                    for k in range(KD):
                        pe(lambda e, slot=slot, pd=pd, t=t, k=k, cgw=cgw:
                           e.matmul(pd[:, 0:cgw], uqT[:, t, k, :],
                                    s4[:, slot, k, 0:cgw],
                                    start=(k == 0), stop=(k == KD - 1)),
                           reads=[("uqT", t), ("s4", slot)], writes=[pk],
                           tag="gumm")
                ho = (cgi % 2) * 512
                act(lambda e, t=t, tp=tp, cgw=cgw, ho=ho: e.activation(
                    out=hst[:, t, ho:ho + cgw], in_=pG[tp][:, 0:cgw],
                    func=AF.Silu),
                    reads=[kG[tp], ("hst", t)], writes=[("hst", t)],
                    tag="silu")
                dve(lambda e, t=t, tp=tp, cgw=cgw, ho=ho: e.tensor_tensor(
                    hst[:, t, ho:ho + cgw], hst[:, t, ho:ho + cgw],
                    pU[tp][:, 0:cgw], op=OP.mult),
                    reads=[kU[tp], ("hst", t)],
                    writes=[("hst", t)], tag="hmul")
                if cgi % 2 == 1 or cgi == len(CGS) - 1:
                    qw = (cgi % 2) * 512 + cgw
                    qlo = cgo + cgw - qw
                    lo = t * I + qlo
                    calls.append(quant_ops(
                        hst[:, t, 0:qw], [("hst", t)], qw,
                        hq_v[:, t, qlo:qlo + qw], s2k(lo, lo + qw),
                        0, tp))
            quant_wave(calls[0:2])
            quant_wave(calls[2:4])
        if dbg and grp == 0 and not is_last and dbg_d.get("dbg_hq") is not None:
            _hq = dbg_d["dbg_hq"]
            out_store_ids.append(pool_dma(
                lambda e, _hq=_hq: e.dma_start(out=_hq[:, :],
                                               in_=hq_v[:, 0, :]),
                reads=s2k(0, I), tag="dbghq"))
            dbg_d["dbg_hq"] = None
        for g in range(NQ // 2):
            ta, tb = 2 * g, 2 * g + 1
            for ti, t in ((0, ta), (1, tb)):
                for k in range(KI):
                    grp2 = (k // 8) % 2
                    pt = pT if grp2 == 0 else pT2
                    pk = ("pT",) if grp2 == 0 else ("pT2",)
                    lo = t * I + k * 128
                    pe(lambda e, k=k, t=t, ti=ti, pt=pt: e.transpose(
                        pt[:, (k % 8) * 128:(k % 8 + 1) * 128],
                        hq_v[:, t, k * 128:(k + 1) * 128], ident[:, :]),
                       reads=s2k(lo, lo + 128) + [("ident",)], writes=[pk],
                       tag="htr")
                    if k % 8 == 7 or k == KI - 1:
                        lo8 = (k // 8) * 8
                        n = k - lo8 + 1
                        act(lambda e, lo8=lo8, n=n, pt=pt, ti=ti: e.copy(
                            hqT[:, ti, lo8:lo8 + n, :].rearrange(
                                "p a b -> p (a b)"), pt[:, 0:n * 128]),
                            reads=[pk, ("hqT", ti)], writes=[("hqT", ti)],
                            tag="htrc")
            for k in range(KI):
                for hi in range(2):
                    pe(lambda e, k=k, hi=hi: e.matmul(
                        pF[:, hi * 512:(hi + 1) * 512], hqT[:, 0, k, :],
                        s1[:, k, hi * 512:(hi + 1) * 512],
                        start=(k == 0), stop=(k == KI - 1)),
                       reads=[("hqT", 0), ("s1",)], writes=[("pF",)],
                       tag="dnmm")
                for hi in range(2):
                    pdb, pkb = (pG[0], kG[0]) if hi == 0 else (pU[0], kU[0])
                    pe(lambda e, k=k, hi=hi, pdb=pdb: e.matmul(
                        pdb[:, 0:512], hqT[:, 1, k, :],
                        s1[:, k, hi * 512:(hi + 1) * 512],
                        start=(k == 0), stop=(k == KI - 1)),
                       reads=[("hqT", 1), ("s1",)], writes=[pkb],
                       tag="dnmm")
            dve(lambda e, ta=ta: e.scalar_tensor_tensor(
                r_sb[:, ta, :], pF[:, :], 0.5, r_sb[:, ta, :],
                op0=OP.mult, op1=OP.add),
                reads=[("pF",), ("r", ta)], writes=[("r", ta)], tag="resid")
            dve(lambda e, tb=tb: e.scalar_tensor_tensor(
                r_sb[:, tb, 0:512], pG[0][:, 0:512], 0.5, r_sb[:, tb, 0:512],
                op0=OP.mult, op1=OP.add),
                reads=[kG[0], ("r", tb)], writes=[("r", tb)], tag="resid")
            dve(lambda e, tb=tb: e.scalar_tensor_tensor(
                r_sb[:, tb, 512:1024], pU[0][:, 0:512], 0.5,
                r_sb[:, tb, 512:1024],
                op0=OP.mult, op1=OP.add),
                reads=[kU[0], ("r", tb)], writes=[("r", tb)], tag="resid")
            if is_last:
                off = grp * NQ * 128
                for t in (ta, tb):
                    oid = pool_dma(lambda e, t=t, off=off: e.dma_start(
                        out=out_d[off + t * 128: off + (t + 1) * 128, :],
                        in_=r_sb[:, t, :]),
                        reads=[("r", t)], writes=[("hbm_out", grp, t)],
                        tag="outst", sem="oq%d" % t)
                    out_store_ids.append(oid)

    def attn_call(grp, dbg_dd=None):
        ln_stats()
        for k in range(KD):
            sp_dma(lambda e, k=k: e.dma_start(
                out=wst[:, 0:C], in_=w_in_d[k * 128:(k + 1) * 128, :]),
                reads=[("wst",)], writes=[("wst",)], tag="awload",
                sem="smq")
            lo = k * C
            dve(lambda e, k=k: e.tensor_copy(win_b[:, k, :], wst[:, 0:C]),
                reads=[("wst",)], writes=s2k(lo, lo + C), tag="awcast")
            sp_dma(lambda e, k=k: e.dma_start(
                out=wst[:, 0:D], in_=w_out_d[k * 128:(k + 1) * 128, :]),
                reads=[("wst",)], writes=[("wst",)], tag="awload",
                sem="smq")
            dve(lambda e, k=k: e.tensor_copy(wout_b[:, k, :], wst[:, 0:D]),
                reads=[("wst",)], writes=[("wout", k), ("wrow",)], tag="awcast")
        for t in range(NQ):
            dve(lambda e, t=t: e.tensor_scalar(
                v_b[:, 0, :], r_sb[:, t, :], mcol[:, t:t + 1], rstd[:, t:t + 1],
                op0=OP.subtract, op1=OP.mult),
                reads=[("r", t), ("mcol",), ("rstd",), ("v", 0)],
                writes=[("v", 0)], tag="lnapply")
            for k in range(KD):
                pe(lambda e, k=k: e.transpose(
                    pT[:, k * 128:(k + 1) * 128],
                    v_b[:, 0, k * 128:(k + 1) * 128], ident[:, :]),
                   reads=[("v", 0), ("ident",)], writes=[("pT",)], tag="vtr")
            act(lambda e: e.copy(hqT[:, 0, 0:8, :].rearrange("p a b -> p (a b)"),
                                 pT[:, :]),
                reads=[("pT",), ("hqT", 0)], writes=[("hqT", 0)], tag="vtrc")
            for j in range(9):
                if j < 4:
                    reg, pk = pG[0][:, (j % 4) * 128:(j % 4 + 1) * 128], kG[0]
                elif j < 8:
                    reg, pk = pU[0][:, (j - 4) * 128:(j - 3) * 128], kU[0]
                else:
                    reg, pk = pF[0:SD, 0:128], ("pF",)
                for k in range(KD):
                    lh_ap = (win_b[:, k, j * 128:(j + 1) * 128] if j < 8
                             else win_b[:, k, D:C])
                    lo = k * C + (j * 128 if j < 8 else D)
                    pe(lambda e, reg=reg, lh_ap=lh_ap, k=k: e.matmul(
                        reg, lh_ap, hqT[:, 0, k, :], start=(k == 0), stop=False),
                       reads=s2k(lo, lo + (128 if j < 8 else SD)) + [("hqT", 0)],
                       writes=[pk], tag="cmbmm")
                b_ap = bin_b[:, j * 128:(j + 1) * 128] if j < 8 else bin_b[:, D:C]
                pe(lambda e, reg=reg, b_ap=b_ap: e.matmul(
                    reg, b_ap, ones_b[:, :], start=False, stop=True),
                   reads=[("bin_b",), ("ones",)], writes=[pk], tag="cmbbias")
            for j in range(8):
                src = (pG[0][:, j * 128:(j + 1) * 128] if j < 4
                       else pU[0][:, (j - 4) * 128:(j - 3) * 128])
                pk = kG[0] if j < 4 else kU[0]
                lo = AW_CMB + j * 128
                act(lambda e, j=j, src=src: e.copy(cmbT[:, j, :], src),
                    reads=[pk] + s2k(lo, lo + 128), writes=s2k(lo, lo + 128),
                    tag="cmbev")
            act(lambda e: e.copy(cmb8_b[:, :], pF[0:SD, 0:128]),
                reads=[("pF",), ("cmb8_b",)], writes=[("cmb8_b",)], tag="cmb8ev")
            dve(lambda e: e.tensor_copy(si_sb[:, :], pF[0:SD, 0:128]),
                reads=[("pF",), ("si_sb",)], writes=[("si_sb",)], tag="siev")
            if dbg_dd is not None and grp == 0 and t == 0:
                for j in range(8):
                    out_store_ids.append(pool_dma(
                        lambda e, j=j: e.dma_start(
                            out=dbg_dd["dbg_cmb"][:, j * 128:(j + 1) * 128],
                            in_=cmbT[:, j, :]),
                        reads=s2k(AW_CMB + j * 128, AW_CMB + (j + 1) * 128),
                        tag="dbgcmb"))
                out_store_ids.append(pool_dma(
                    lambda e: e.dma_start(
                        out=dbg_dd["dbg_cmb"][0:SD, 8 * 128:9 * 128],
                        in_=cmb8_b[:, :]),
                    reads=[("cmb8_b",)], tag="dbgcmb8"))
            greg = pF[0:SD, 128:256]
            for j in range(8):
                lo = AW_CMB + j * 128
                pe(lambda e, j=j, greg=greg: e.matmul(
                    greg, wg_b[:, j, :], cmbT[:, j, :],
                    start=(j == 0), stop=False),
                   reads=[("wg_b",)] + s2k(lo, lo + 128), writes=[("pF",)],
                   tag="gmm")
            pe(lambda e, greg=greg: e.matmul(
                greg, wg8_b[:, :], cmb8_b[:, :], start=False, stop=False),
               reads=[("wg8_b",), ("cmb8_b",)], writes=[("pF",)], tag="gmm8")
            pe(lambda e, greg=greg: e.matmul(
                greg, bg_b[:, :], ones_b[:, :], start=False, stop=True),
               reads=[("bg_b",), ("ones",)], writes=[("pF",)], tag="gbias")
            act(lambda e, greg=greg: e.activation(
                out=g_sb[:, :], in_=greg, func=AF.Sigmoid),
                reads=[("pF",), ("g_sb",)], writes=[("g_sb",)], tag="sigm")
            dve(lambda e: e.tensor_scalar(states[:, :], g_sb[:, :], -1.0, 1.0,
                                          op0=OP.mult, op1=OP.add),
                reads=[("g_sb",)], writes=[("states",)])
            dve(lambda e: e.tensor_tensor(si_sb[:, :], states[:, :],
                                          si_sb[:, :], op=OP.mult),
                reads=[("states",), ("si_sb",)], writes=[("si_sb",)])
            dve(lambda e: e.tensor_tensor_scan(
                states[:, :], g_sb[:, :], si_sb[:, :], carry[:, :],
                op0=OP.mult, op1=OP.add),
                reads=[("g_sb",), ("si_sb",), ("carry",)],
                writes=[("states",)], tag="scan")
            dve(lambda e: e.tensor_copy(carry[:, :], states[:, 127:128]),
                reads=[("states",)], writes=[("carry",)])
            dve(lambda e: e.tensor_copy(states_b[:, :], states[:, :]),
                reads=[("states",)], writes=[("states_b",)])
            if dbg_dd is not None:
                out_store_ids.append(pool_dma(
                    lambda e, t=t, grp=grp: e.dma_start(
                        out=dbg_dd["dbg_st"][:, (grp * NQ + t) * 128:
                                             (grp * NQ + t + 1) * 128],
                        in_=states[:, :]),
                    reads=[("states",)], tag="dbgst"))
            for a in (0, 512):
                for j in range(8):
                    pe(lambda e, j=j, a=a: e.matmul(
                        pF[:, a:a + 512], cmbT[:, j, :],
                        wout_b[:, j, a:a + 512], start=(j == 0), stop=False),
                       reads=s2k(AW_CMB + j * 128, AW_CMB + (j + 1) * 128)
                       + [("wout", j)], writes=[("pF",)], tag="omm")
                pe(lambda e, a=a: e.matmul(
                    pF[:, a:a + 512], states_b[:, :], w8o_b[:, a:a + 512],
                    start=False, stop=False),
                   reads=[("states_b",), ("w8o_b",)], writes=[("pF",)],
                   tag="omm8")
                pe(lambda e, a=a: e.matmul(
                    pF[:, a:a + 512], ones_b[:, :], bout_b[:, a:a + 512],
                    start=False, stop=True),
                   reads=[("ones",), ("bout_b",)], writes=[("pF",)],
                   tag="obias")
            dve(lambda e, t=t: e.tensor_tensor(
                r_sb[:, t, :], pF[:, :], r_sb[:, t, :], op=OP.add),
                reads=[("pF",), ("r", t)], writes=[("r", t)], tag="aresid")

    for grp in range(NGRP):
        for t in range(NQ):
            off = grp * NQ * 128
            sp_dma(lambda e, t=t, off=off: e.dma_start(
                out=r_sb[:, t, :],
                in_=x_d[off + t * 128: off + (t + 1) * 128, :]),
                reads=[("r", t)], writes=[("r", t)], tag="xload",
                sem="xq%d" % t)
        ffn_call(grp, is_last=False)
        if dbg:
            for t in range(NQ):
                off = grp * NQ * 128
                out_store_ids.append(pool_dma(
                    lambda e, t=t, off=off: e.dma_start(
                        out=dbg_d["dbg_r1"][off + t * 128: off + (t + 1) * 128, :],
                        in_=r_sb[:, t, :]),
                    reads=[("r", t)], tag="dbgr1"))
        attn_call(grp, dbg_d if dbg else None)
        if dbg:
            for t in range(NQ):
                off = grp * NQ * 128
                out_store_ids.append(pool_dma(
                    lambda e, t=t, off=off: e.dma_start(
                        out=dbg_d["dbg_r2"][off + t * 128: off + (t + 1) * 128, :],
                        in_=r_sb[:, t, :]),
                    reads=[("r", t)], tag="dbgr2"))
        ffn_call(grp, is_last=True)

    P.plan(force_need=out_store_ids)
    sem_names = sorted({op["sem"] for op in P.ops if op["need"]})
    sems = {nm: es.enter_context(nc.semaphore(nm)) for nm in sem_names}
    with nc.Block() as block:
        @block.sync
        def _(eng):
            P.emit_stream(eng, "sp", sems)

        @block.vector
        def _(eng):
            P.emit_stream(eng, "dve", sems)

        @block.scalar
        def _(eng):
            P.emit_stream(eng, "act", sems)

        @block.tensor
        def _(eng):
            P.emit_stream(eng, "pe", sems)

        @block.gpsimd
        def _(eng):
            P.emit_stream(eng, "pool", sems)
            for qn, tot in P.sem_totals.items():
                if qn in sems and tot and P.ops and any(
                        o["sem"] == qn and o["dma"] for o in P.ops):
                    eng.wait_ge(sems[qn], tot * 16)
    es.close()
    return nc


_NC_CACHE = None


def kernel(L0_attn_x, L0_attn_W_in, L0_attn_b_in, L0_attn_W_g, L0_attn_b_g,
           L0_attn_W_out, L0_attn_b_out, L0_ffn_w_gate, L0_ffn_w_up,
           L0_ffn_w_down):
    global _NC_CACHE
    if _NC_CACHE is None:
        _NC_CACHE = build_nc()
    nc = _NC_CACHE
    f32 = lambda a: np.ascontiguousarray(np.asarray(a, dtype=np.float32))
    shared = {
        "w_in": f32(L0_attn_W_in),
        "b_in": f32(L0_attn_b_in).reshape(1, C),
        "w_g": f32(L0_attn_W_g),
        "b_g": f32(L0_attn_b_g).reshape(1, SD),
        "w_out": f32(L0_attn_W_out),
        "b_out": f32(L0_attn_b_out).reshape(1, D),
        "w_gate": f32(L0_ffn_w_gate),
        "w_up": f32(L0_ffn_w_up),
        "w_down": f32(L0_ffn_w_down),
    }
    in_maps = []
    for c in range(8):
        m = dict(shared)
        m["x"] = f32(L0_attn_x[c])
        in_maps.append(m)
    res = run_bass_kernel_spmd(nc, in_maps, core_ids=list(range(8)))
    return np.stack([res.results[c]["out"] for c in range(8)], axis=0)


# version 23 (final)
# speedup vs baseline: 1.0119x; 1.0119x over previous
import numpy as np
from contextlib import ExitStack

import concourse.bass as bass
import concourse.mybir as mybir
from concourse.bass_utils import run_bass_kernel_spmd

F32 = mybir.dt.float32
BF16 = mybir.dt.bfloat16
AF = mybir.ActivationFunctionType
OP = mybir.AluOpType
AX = mybir.AxisListType

T = 2048
D = 1024
C = 1032
SD = 8
I = 2816
NTH = 8
CG = 704
NCG = I // CG
KD = D // 128
KI = I // 128
EPS = 1e-5


class Prog:

    def __init__(self):
        self.ops = []
        self.last_writer = {}
        self.readers = {}

    def add(self, stream, fn, reads=(), writes=(), dma=False, tag="",
            big=False, sem=None):
        idx = len(self.ops)
        deps = set()
        for k in reads:
            j = self.last_writer.get(k)
            if j is not None:
                deps.add(j)
        for k in writes:
            j = self.last_writer.get(k)
            if j is not None:
                deps.add(j)
            deps.update(self.readers.get(k, ()))
        deps.discard(idx)
        self.ops.append(dict(stream=stream, fn=fn,
                             sem=sem or (stream + ("q" if dma else "")),
                             step=16 if dma else 1, deps=deps, dma=dma,
                             tag=tag, need=False, rank=None, waits=None,
                             big=big, drain=False))
        for k in reads:
            self.readers.setdefault(k, []).append(idx)
        for k in writes:
            self.last_writer[k] = idx
            self.readers[k] = []
        return idx

    def plan(self, force_need=()):
        ops = self.ops
        for op in ops:
            for j in op["deps"]:
                dj = ops[j]
                if dj["stream"] != op["stream"] or dj["dma"]:
                    dj["need"] = True
                if (not dj["dma"] and dj["stream"] not in ("pe", "sp")
                        and (not dj["big"]
                             or dj["stream"] in ("dve", "pool"))):
                    if dj["stream"] in ("dve", "pool"):
                        dj["need"] = True
                    else:
                        dj["drain"] = True
        for i in force_need:
            ops[i]["need"] = True
        forced = set(force_need)
        for i, op in enumerate(ops):
            if op["dma"]:
                op["need"] = True
                forced.add(i)
        sem_ops = {}
        for i, op in enumerate(ops):
            sem_ops.setdefault(op["sem"], []).append(i)
        import bisect
        for _ in range(6):
            counters = {}
            for op in ops:
                op["rank"] = None
            for sem, lst in sem_ops.items():
                r = 0
                for i in lst:
                    if ops[i]["need"]:
                        r += 1
                        ops[i]["rank"] = r
                counters[sem] = r
            know_stream = {}
            know_op = {}
            wait_targets = set()
            for i, op in enumerate(ops):
                s = op["stream"]
                know = know_stream.setdefault(s, {})
                req = {}
                for j in sorted(op["deps"]):
                    dj = ops[j]
                    if (dj["stream"] == s and not dj["dma"]
                            and s not in ("pool", "dve")):
                        continue
                    tj = j
                    if not dj["need"]:
                        lst = sem_ops[dj["sem"]]
                        p = bisect.bisect_left(lst, j)
                        tj = None
                        while p < len(lst):
                            if ops[lst[p]]["need"]:
                                tj = lst[p]
                                break
                            p += 1
                        assert tj is not None, (i, j, dj["tag"])
                    dt = ops[tj]
                    cur = req.get(dt["sem"])
                    if cur is None or dt["rank"] > cur[0]:
                        req[dt["sem"]] = (dt["rank"], tj)
                waits = []
                for sem, (rank, tj) in req.items():
                    if know.get(sem, 0) >= rank:
                        continue
                    dt = ops[tj]
                    waits.append((sem, rank * dt["step"]))
                    wait_targets.add(tj)
                    know[sem] = rank
                    for sem2, r2 in know_op.get(tj, {}).items():
                        if know.get(sem2, 0) < r2:
                            know[sem2] = r2
                op["waits"] = waits
                snap = dict(know)
                if op["need"]:
                    snap[op["sem"]] = max(snap.get(op["sem"], 0), op["rank"])
                know_op[i] = snap
            new_need = wait_targets | forced
            old_need = {i for i, op in enumerate(ops) if op["need"]}
            if new_need == old_need:
                break
            for i, op in enumerate(ops):
                op["need"] = i in new_need
        self.sem_totals = counters

    def emit_stream(self, eng, stream, sems):
        for op in self.ops:
            if op["stream"] != stream:
                continue
            for sem_name, val in op["waits"]:
                eng.wait_ge(sems[sem_name], val)
            instr = op["fn"](eng)
            if op["drain"]:
                instr = eng.drain()
            if op["need"]:
                instr.then_inc(sems[op["sem"]], op["step"])


def build_nc(dbg=False):
    nc = bass.Bass("TRN2", target_bir_lowering=False)

    x_d = nc.dram_tensor("x", [T, D], F32, kind="ExternalInput").ap()
    w_in_d = nc.dram_tensor("w_in", [D, C], F32, kind="ExternalInput").ap()
    b_in_d = nc.dram_tensor("b_in", [1, C], F32, kind="ExternalInput").ap()
    w_g_d = nc.dram_tensor("w_g", [C, SD], F32, kind="ExternalInput").ap()
    b_g_d = nc.dram_tensor("b_g", [1, SD], F32, kind="ExternalInput").ap()
    w_out_d = nc.dram_tensor("w_out", [C, D], F32, kind="ExternalInput").ap()
    b_out_d = nc.dram_tensor("b_out", [1, D], F32, kind="ExternalInput").ap()
    w_gate_d = nc.dram_tensor("w_gate", [I, D], F32, kind="ExternalInput").ap()
    w_up_d = nc.dram_tensor("w_up", [I, D], F32, kind="ExternalInput").ap()
    w_down_d = nc.dram_tensor("w_down", [D, I], F32, kind="ExternalInput").ap()
    out_d = nc.dram_tensor("out", [T, D], F32, kind="ExternalOutput").ap()
    wqT_g_d = nc.dram_tensor("wqT_g", [D, I], BF16, kind="Internal").ap()
    wqT_u_d = nc.dram_tensor("wqT_u", [D, I], BF16, kind="Internal").ap()
    wqT_d_d = nc.dram_tensor("wqT_d", [I, D], BF16, kind="Internal").ap()
    dbg_d = {}
    if dbg:
        for nm, shp, dt in [("dbg_uq", [128, D], BF16),
                            ("dbg_wg", [D, I], BF16),
                            ("dbg_wd", [I, D], BF16),
                            ("dbg_hq", [128, I], BF16),
                            ("dbg_r1", [T, D], F32),
                            ("dbg_r2", [T, D], F32),
                            ("dbg_cmb", [128, 9 * 128], F32),
                            ("dbg_st", [SD, T], F32)]:
            dbg_d[nm] = nc.dram_tensor(nm, shp, dt, kind="ExternalOutput").ap()

    es = ExitStack()
    sb = lambda name, shape, dt: es.enter_context(nc.sbuf_tensor(name, shape, dt))
    ps = lambda name, shape, dt: es.enter_context(nc.psum_tensor(name, shape, dt))

    NQ = 4
    NGRP = (T // 128) // NQ
    QW = 1024
    CGS = []
    o = 0
    while o < I:
        w = min(512, I - o)
        CGS.append((o, w))
        o += w

    r_sb = sb("r", [128, NQ, D], F32)
    s1 = sb("s1", [128, KI, D], BF16)
    s2 = sb("s2", [128, NQ * I], BF16)
    NPAR = 2
    b1 = sb("b1", [128, NPAR, QW], F32)
    b2 = sb("b2", [128, NPAR, QW], F32)
    b3 = sb("b3", [128, NPAR, QW], BF16)
    b4 = sb("b4", [128, NPAR, QW], BF16)
    b5 = sb("b5", [128, NPAR, QW], BF16)
    hst = sb("hst", [128, 4, QW], F32)
    wstp = hst[:, 0:2, :]
    s4 = sb("s4", [128, 3, KD, 512], BF16)
    uqT = sb("uqT", [128, NQ, KD, 128], BF16)
    hqT = sb("hqT", [128, 2, KI, 128], BF16)
    u_b = sb("u", [128, D], F32)
    v_b = sb("v", [128, 2, D], BF16)
    wst = sb("wst", [128, C], F32)
    big16 = sb("big16", [128, 8192], BF16)
    wrow = big16[:, 0:I]
    ident = sb("ident", [128, 128], BF16)
    ones_b = sb("ones", [1, 128], BF16)
    bin_b = sb("bin_b", [1, C], BF16)
    bg_b = sb("bg_b", [1, SD], BF16)
    bout_b = sb("bout_b", [1, D], BF16)
    wg_b = sb("wg_b", [128, KD, SD], BF16)
    wg8_b = sb("wg8_b", [SD, SD], BF16)
    w8o_b = sb("w8o_b", [SD, D], BF16)
    sums = sb("sums", [128, NQ], F32)
    sqs = sb("sqs", [128, NQ], F32)
    mcol = sb("mcol", [128, NQ], F32)
    varc = sb("varc", [128, NQ], F32)
    stdc = sb("stdc", [128, NQ], F32)
    rstd = sb("rstd", [128, NQ], F32)
    msq = sb("msq", [128, NQ], F32)
    NG = I // 16
    gmax = sb("gmax", [128, NPAR, QW // 16], F32)
    scal = sb("scal", [128, NPAR, QW // 16], F32)
    rinv = sb("rinv", [128, NPAR, QW // 16], F32)
    g_sb = sb("g_sb", [SD, 128], F32)
    si_sb = sb("si_sb", [SD, 128], F32)
    states = sb("states", [SD, 128], F32)
    states_b = sb("states_b", [SD, 128], BF16)
    carry = sb("carry", [SD, 1], F32)
    cmb8_b = sb("cmb8_b", [SD, 128], BF16)
    eps_c = sb("eps_c", [128, 1], F32)
    nb2 = sb("nb2", [128, 1], F32)

    AW_CMB = KD * C
    s2f = s2[:, :]
    win_b = s2f[:, 0:KD * C].rearrange("p (k c) -> p k c", c=C)
    wout_b = big16[:, :].rearrange("p (k c) -> p k c", c=D)
    cmbT = s2f[:, AW_CMB:AW_CMB + 8 * 128].rearrange("p (j c) -> p j c", c=128)
    hq_v = s2f.rearrange("p (t i) -> p t i", i=I)
    wq_v = s2f.rearrange("p (r w) -> p r w", w=D)
    wqd_v = s2f.rearrange("p (r w) -> p r w", w=I)

    def s2k(lo, hi):
        return [("s2", p) for p in range(lo // I, (hi - 1) // I + 1)]

    pG = [ps("pG0", [128, 512], F32), ps("pG1", [128, 512], F32)]
    pU = [ps("pU0", [128, 512], F32), ps("pU1", [128, 512], F32)]
    pF = ps("pF", [128, D], F32)
    pT = ps("pT", [128, 1024], BF16)
    pT2 = ps("pT2", [128, 1024], BF16)
    kG = [("pG0",), ("pG1",)]
    kU = [("pU0",), ("pU1",)]

    P = Prog()
    BIG_TAGS = {"qnorm", "qsgn", "qsm", "qout", "qk1", "qk2", "qk3",
                "qc1", "qt2", "qu1", "qt3", "qq2", "lnsq",
                "lnapply", "silu", "hmul", "resid", "aresid", "utrc", "htrc",
                "vtrc", "wtrc", "wtos1", "awcast", "smcast"}
    dve = lambda fn, reads=(), writes=(), tag="": P.add(
        "dve", fn, reads, writes, tag=tag, big=tag in BIG_TAGS)
    act = lambda fn, reads=(), writes=(), tag="": P.add(
        "act", fn, reads, writes, tag=tag, big=tag in BIG_TAGS)
    pool = lambda fn, reads=(), writes=(), tag="": P.add(
        "pool", fn, reads, writes, tag=tag, big=tag in BIG_TAGS)
    pe = lambda fn, reads=(), writes=(), tag="": P.add("pe", fn, reads, writes, tag=tag)
    sp_dma = lambda fn, reads=(), writes=(), tag="", sem=None: P.add(
        "sp", fn, reads, writes, dma=True, tag=tag, sem=sem)
    pool_dma = lambda fn, reads=(), writes=(), tag="", sem=None: P.add(
        "pool", fn, reads, writes, dma=True, tag=tag, sem=sem)

    out_store_ids = []

    pool(lambda e: e.memset(ident[:, :], 0.0), writes=[("ident",)])
    pool(lambda e: e.affine_select(
        out=ident[:, :], in_=ident[:, :], compare_op=OP.not_equal,
        fill=1.0, base=0, pattern=[[-1, 128]], channel_multiplier=1),
        reads=[("ident",)], writes=[("ident",)])
    pool(lambda e: e.memset(ones_b[:, :], 1.0), writes=[("ones",)])
    pool(lambda e: e.memset(carry[:, :], 0.0), writes=[("carry",)])
    pool(lambda e: e.memset(eps_c[:, :], EPS), writes=[("eps_c",)])
    pool(lambda e: e.memset(nb2[:, :], -2.0), writes=[("nb2",)])

    def small_load(dst_ap, dst_key, src, rows, cols):
        sp_dma(lambda e: e.dma_start(out=wst[0:rows, 0:cols], in_=src),
               reads=[("wst",)], writes=[("wst",)], tag="smload", sem="smq")
        dve(lambda e: e.tensor_copy(dst_ap, wst[0:rows, 0:cols]),
            reads=[("wst",)], writes=[dst_key], tag="smcast")

    small_load(bin_b[:, :], ("bin_b",), b_in_d[:, :], 1, C)
    small_load(bg_b[:, :], ("bg_b",), b_g_d[:, :], 1, SD)
    small_load(bout_b[:, :], ("bout_b",), b_out_d[:, :], 1, D)
    sp_dma(lambda e: e.dma_start(
        out=wst[:, 0:KD * SD].rearrange("p (k s) -> p k s", s=SD),
        in_=w_g_d[0:D, :].rearrange("(k p) s -> p k s", p=128)),
        reads=[("wst",)], writes=[("wst",)], tag="smload", sem="smq")
    dve(lambda e: e.tensor_copy(
        wg_b[:, :, :], wst[:, 0:KD * SD].rearrange("p (k s) -> p k s", s=SD)),
        reads=[("wst",)], writes=[("wg_b",)], tag="smcast")
    small_load(wg8_b[:, :], ("wg8_b",), w_g_d[D:C, :], SD, SD)
    small_load(w8o_b[:, :], ("w8o_b",), w_out_d[D:C, :], SD, D)

    I32 = mybir.dt.int32
    b1i = b1[:, :, :].bitcast(I32)
    b2i = b2[:, :, :].bitcast(I32)
    b2h = b2[:, :, :].bitcast(BF16)

    def quant_ops(src_ap, src_keys, width, out_ap, out_keys, goff, par,
                  half=False):
        assert width <= QW and width % 16 == 0
        G = width // 16
        gm = gmax[:, par, 0:G]
        sc = scal[:, par, 0:G]
        ri = rinv[:, par, 0:G]
        src_g = src_ap.rearrange("p (g k) -> p g k", k=16)
        vv = b1[:, par, 0:width]
        kk = b2i[:, par, 0:width]
        sg = b2h[:, par, 0:width]
        c1 = b3[:, par, 0:width]
        c2 = b4[:, par, 0:width]
        c3 = b5[:, par, 0:width]
        w2 = b1[:, par, 0:width]
        v_g = vv.rearrange("p (g k) -> p g k", k=16)
        w2g = w2.rearrange("p (g k) -> p g k", k=16)
        sg_g = sg.rearrange("p (g k) -> p g k", k=16)
        rib = ri.unsqueeze(2).broadcast_to([128, G, 16])
        scb = sc.unsqueeze(2).broadcast_to([128, G, 16])
        kq1, kq2 = ("qb1", par), ("qb2", par)
        kc1, kc2, kc3 = ("qb3", par), ("qb4", par), ("qb5", par)
        kgm, ksc, kri = ("gmax", par), ("scal", par), ("rinv", par)
        src_keys = list(src_keys)
        out_keys = list(out_keys)
        return [
            (dve, lambda e: e.tensor_reduce(gm, src_g, axis=AX.X, op=OP.max,
                                            apply_absolute_value=True),
             src_keys, [kgm], "qred"),
            (dve, lambda e: e.tensor_scalar(sc, gm, 1.0 / 6.0, 1e-12,
                                            op0=OP.mult, op1=OP.max),
             [kgm], [ksc], "qsc"),
            (dve, lambda e: e.reciprocal(ri, sc), [ksc], [kri], "qri"),
        ] + ([
            (dve, lambda e: e.tensor_scalar(sc, sc, 0.5, 0.0,
                                            op0=OP.mult, op1=OP.add),
             [ksc, kri], [ksc], "qhlf"),
        ] if half else []) + [
            (dve, lambda e: e.tensor_tensor(v_g, src_g, rib, op=OP.mult),
             src_keys + [kri], [kq1], "qnorm"),
            (act, lambda e: e.activation(out=kk, in_=vv, func=AF.Abs,
                                         scale=2.0),
             [kq1], [kq2], "qk1"),
            (dve, lambda e: e.tensor_scalar(c1, kk, 4.0, 0.5,
                                            op0=OP.min, op1=OP.mult),
             [kq2], [kc1], "qc1"),
            (act, lambda e: e.activation(out=kk, in_=vv, func=AF.Abs,
                                         scale=1.0),
             [kq1, kq2], [kq2], "qk2"),
            (dve, lambda e: e.tensor_scalar(c2, kk, 2.0, 0.0,
                                            op0=OP.subtract, op1=OP.max),
             [kq2], [kc2], "qt2"),
            (dve, lambda e: e.scalar_tensor_tensor(c2, c2, 2.0, c1,
                                                   op0=OP.min, op1=OP.add),
             [kc2, kc1], [kc2], "qu1"),
            (act, lambda e: e.activation(out=kk, in_=vv, func=AF.Abs,
                                         scale=0.5),
             [kq1, kq2], [kq2], "qk3"),
            (dve, lambda e: e.tensor_scalar(c3, kk, 2.0, 0.0,
                                            op0=OP.subtract, op1=OP.max),
             [kq2], [kc3], "qt3"),
            (dve, lambda e: e.scalar_tensor_tensor(c1, c3, 2.0, c2,
                                                   op0=OP.mult, op1=OP.add),
             [kc3, kc2, kc1], [kc1], "qq2"),
            (act, lambda e: e.activation(out=sg, in_=vv, func=AF.Sign),
             [kq1, kq2], [kq2], "qsgn"),
            (dve, lambda e: e.tensor_tensor(w2g, sg_g, scb, op=OP.mult),
             [kq2, ksc, kq1], [kq1], "qsm"),
            (pool, lambda e: e.tensor_tensor(out_ap, c1, w2, op=OP.mult),
             [kc1, kq1], out_keys, "qout"),
        ]

    def quant_wave(call_list):
        if not call_list:
            return
        nst = len(call_list[0])
        for s in range(nst):
            for ops in call_list:
                adder, fn, reads, writes, tag = ops[s]
                adder(fn, reads=reads, writes=writes, tag=tag)

    def quant(src_ap, src_keys, width, out_ap, out_keys, goff, par):
        quant_wave([quant_ops(src_ap, src_keys, width, out_ap, out_keys,
                              goff, par)])

    def wq_weight2(w_d, nrt, wcols, to_hbm_ap, hbm_key, wsem,
                   whalf=False):
        stg_v = wq_v if wcols == D else wqd_v
        rows_fit = (NQ * I) // wcols
        pq = 0
        for rbase in range(0, nrt, rows_fit):
            rtop = min(rbase + rows_fit, nrt)
            subs = []
            for r in range(rbase, rtop):
                off = 0
                while off < wcols:
                    sw = min(QW, wcols - off)
                    subs.append((r, off, sw))
                    off += sw
            for wv in range(0, len(subs), 2):
                wave = subs[wv:wv + 2]
                calls = []
                for wi2, (r, off, sw) in enumerate(wave):
                    wp = wi2
                    sp_dma(lambda e, r=r, off=off, sw=sw, wp=wp: e.dma_start(
                        out=wstp[:, wp, 0:sw],
                        in_=w_d[r * 128:(r + 1) * 128, off:off + sw]),
                        reads=[("hst", wp)], writes=[("hst", wp)],
                        tag="wload", sem="wlq%d" % wp)
                    sr = r - rbase
                    lo = sr * wcols + off
                    calls.append(quant_ops(
                        wstp[:, wp, 0:sw], [("hst", wp)], sw,
                        stg_v[:, sr, off:off + sw], s2k(lo, lo + sw),
                        off // 16, wp, half=whalf))
                quant_wave(calls)
                pq += len(wave)
            nkc = wcols // 128
            for k in range(nkc):
                for r in range(rbase, rtop):
                    sr = r - rbase
                    grp = (sr // 8) % 2
                    pt = pT if grp == 0 else pT2
                    pk = ("pT",) if grp == 0 else ("pT2",)
                    lo = sr * wcols + k * 128
                    pe(lambda e, k=k, sr=sr, pt=pt: e.transpose(
                        pt[:, (sr % 8) * 128:(sr % 8 + 1) * 128],
                        stg_v[:, sr, k * 128:(k + 1) * 128], ident[:, :]),
                       reads=s2k(lo, lo + 128) + [("ident",)], writes=[pk],
                       tag="wtr")
                    if sr % 8 == 7 or r == rtop - 1:
                        lo8 = (sr // 8) * 8
                        n = sr - lo8 + 1
                        act(lambda e, lo8=lo8, n=n, pt=pt, rbase=rbase: e.copy(
                            wrow[:, (rbase + lo8) * 128:(rbase + lo8 + n) * 128],
                            pt[:, 0:n * 128]),
                            reads=[pk, ("wrow",)], writes=[("wrow",)], tag="wtrc")
                sp_dma(lambda e, k=k, rbase=rbase, rtop=rtop: e.dma_start(
                    out=to_hbm_ap[k * 128:(k + 1) * 128,
                                  rbase * 128:rtop * 128],
                    in_=wrow[:, rbase * 128:rtop * 128]),
                    reads=[("wrow",)], writes=[hbm_key], tag="wstore",
                    sem=wsem)

    wq_weight2(w_gate_d, KI, D, wqT_g_d, ("hbm_wg",), "wsg")
    wq_weight2(w_up_d, KI, D, wqT_u_d, ("hbm_wu",), "wsu")
    wq_weight2(w_down_d, KD, I, wqT_d_d, ("hbm_wd",), "wsd",
               whalf=True)

    if dbg:
        out_store_ids.append(pool_dma(
            lambda e: e.dma_start(out=dbg_d["dbg_wg"][:, :], in_=wqT_g_d[:, :]),
            reads=[("hbm_wg",)], tag="dbgwg"))
        out_store_ids.append(pool_dma(
            lambda e: e.dma_start(out=dbg_d["dbg_wd"][:, :], in_=wqT_d_d[:, :]),
            reads=[("hbm_wd",)], tag="dbgwd"))

    for kk in range(0, KI, 6):
        kn = min(6, KI - kk)
        sp_dma(lambda e, kk=kk, kn=kn: e.dma_start(
            out=s1[:, kk:kk + kn, :],
            in_=wqT_d_d[kk * 128:(kk + kn) * 128, :].rearrange(
                "(k p) n -> p k n", p=128)),
            reads=[("hbm_wd",), ("s1",)], writes=[("s1",)], tag="s1load",
            sem="slq")

    def ln_stats():
        for t in range(NQ):
            dve(lambda e, t=t: e.tensor_reduce(
                sums[:, t:t + 1], r_sb[:, t, :], axis=AX.X, op=OP.add),
                reads=[("r", t)], writes=[("sums",)], tag="lnsum")
            act(lambda e, t=t: e.activation(
                out=u_b[:, :], in_=r_sb[:, t, :], func=AF.Square),
                reads=[("r", t), ("u",)], writes=[("u",)], tag="lnsq")
            dve(lambda e, t=t: e.tensor_reduce(
                sqs[:, t:t + 1], u_b[:, :], axis=AX.X, op=OP.add),
                reads=[("u",)], writes=[("sqs",)], tag="lnsq2")
        dve(lambda e: e.tensor_scalar(mcol[:, :], sums[:, :], 1.0 / D, 0.0,
                                      op0=OP.mult, op1=OP.add),
            reads=[("sums",)], writes=[("mcol",)])
        dve(lambda e: e.tensor_scalar(varc[:, :], sqs[:, :], 1.0 / D, 0.0,
                                      op0=OP.mult, op1=OP.add),
            reads=[("sqs",)], writes=[("varc",)])
        dve(lambda e: e.tensor_tensor(msq[:, :], mcol[:, :], mcol[:, :],
                                      op=OP.mult),
            reads=[("mcol",)], writes=[("msq",)])
        dve(lambda e: e.tensor_tensor(varc[:, :], varc[:, :], msq[:, :],
                                      op=OP.subtract),
            reads=[("varc",), ("msq",)], writes=[("varc",)])
        act(lambda e: e.activation(out=stdc[:, :], in_=varc[:, :],
                                   func=AF.Sqrt, bias=eps_c[:, :], scale=1.0),
            reads=[("varc",), ("eps_c",)], writes=[("stdc",)], tag="lnsqrt")
        dve(lambda e: e.reciprocal(rstd[:, :], stdc[:, :]),
            reads=[("stdc",)], writes=[("rstd",)])

    def ffn_call(grp, is_last):
        ln_stats()
        for t in range(NQ):
            dve(lambda e, t=t: e.tensor_scalar(
                u_b[:, :], r_sb[:, t, :], mcol[:, t:t + 1], rstd[:, t:t + 1],
                op0=OP.subtract, op1=OP.mult),
                reads=[("r", t), ("mcol",), ("rstd",), ("u",)], writes=[("u",)],
                tag="lnapply")
            quant_wave([quant_ops(
                u_b[:, :], [("u",)], D,
                v_b[:, t % 2, :], [("v", t % 2)], 0, t % 2)])
            if dbg and grp == 0 and t == 0 and not is_last and dbg_d.get("dbg_uq") is not None:
                _a = dbg_d["dbg_uq"]
                out_store_ids.append(pool_dma(
                    lambda e, _a=_a, t=t: e.dma_start(out=_a[:, :], in_=v_b[:, t % 2, :]),
                    reads=[("v", t % 2)], tag="dbguq"))
                dbg_d["dbg_uq"] = None
            for k in range(KD):
                pe(lambda e, k=k, t=t: e.transpose(
                    pT[:, k * 128:(k + 1) * 128],
                    v_b[:, t % 2, k * 128:(k + 1) * 128], ident[:, :]),
                   reads=[("v", t % 2), ("ident",)], writes=[("pT",)], tag="utr")
            act(lambda e, t=t: e.copy(
                uqT[:, t, :, :].rearrange("p a b -> p (a b)"), pT[:, :]),
                reads=[("pT",), ("uqT", t)], writes=[("uqT", t)], tag="utrc")
        for cgi, (cgo, cgw) in enumerate(CGS):
            slot_g = (2 * cgi) % 3
            slot_u = (2 * cgi + 1) % 3
            for slot, wsrc, hk in ((slot_g, wqT_g_d, ("hbm_wg",)),
                                   (slot_u, wqT_u_d, ("hbm_wu",))):
                sp_dma(lambda e, slot=slot, wsrc=wsrc, cgo=cgo, cgw=cgw:
                       e.dma_start(
                           out=s4[:, slot, :, 0:cgw],
                           in_=wsrc[:, cgo:cgo + cgw].rearrange(
                               "(k p) n -> p k n", p=128)),
                       reads=[("s4", slot)], writes=[("s4", slot)],
                       tag="wchunk", sem="wcq%d" % slot)
            calls = []
            for t in range(NQ):
                tp = t % 2
                for slot, pd, pk in ((slot_g, pG[tp], kG[tp]),
                                     (slot_u, pU[tp], kU[tp])):
                    for k in range(KD):
                        pe(lambda e, slot=slot, pd=pd, t=t, k=k, cgw=cgw:
                           e.matmul(pd[:, 0:cgw], uqT[:, t, k, :],
                                    s4[:, slot, k, 0:cgw],
                                    start=(k == 0), stop=(k == KD - 1)),
                           reads=[("uqT", t), ("s4", slot)], writes=[pk],
                           tag="gumm")
                ho = (cgi % 2) * 512
                act(lambda e, t=t, tp=tp, cgw=cgw, ho=ho: e.activation(
                    out=hst[:, t, ho:ho + cgw], in_=pG[tp][:, 0:cgw],
                    func=AF.Silu),
                    reads=[kG[tp], ("hst", t)], writes=[("hst", t)],
                    tag="silu")
                dve(lambda e, t=t, tp=tp, cgw=cgw, ho=ho: e.tensor_tensor(
                    hst[:, t, ho:ho + cgw], hst[:, t, ho:ho + cgw],
                    pU[tp][:, 0:cgw], op=OP.mult),
                    reads=[kU[tp], ("hst", t)],
                    writes=[("hst", t)], tag="hmul")
                if cgi % 2 == 1 or cgi == len(CGS) - 1:
                    qw = (cgi % 2) * 512 + cgw
                    qlo = cgo + cgw - qw
                    lo = t * I + qlo
                    calls.append(quant_ops(
                        hst[:, t, 0:qw], [("hst", t)], qw,
                        hq_v[:, t, qlo:qlo + qw], s2k(lo, lo + qw),
                        0, tp))
            quant_wave(calls[0:2])
            quant_wave(calls[2:4])
        if dbg and grp == 0 and not is_last and dbg_d.get("dbg_hq") is not None:
            _hq = dbg_d["dbg_hq"]
            out_store_ids.append(pool_dma(
                lambda e, _hq=_hq: e.dma_start(out=_hq[:, :],
                                               in_=hq_v[:, 0, :]),
                reads=s2k(0, I), tag="dbghq"))
            dbg_d["dbg_hq"] = None
        for g in range(NQ // 2):
            ta, tb = 2 * g, 2 * g + 1
            for ti, t in ((0, ta), (1, tb)):
                for k in range(KI):
                    grp2 = (k // 8) % 2
                    pt = pT if grp2 == 0 else pT2
                    pk = ("pT",) if grp2 == 0 else ("pT2",)
                    lo = t * I + k * 128
                    pe(lambda e, k=k, t=t, ti=ti, pt=pt: e.transpose(
                        pt[:, (k % 8) * 128:(k % 8 + 1) * 128],
                        hq_v[:, t, k * 128:(k + 1) * 128], ident[:, :]),
                       reads=s2k(lo, lo + 128) + [("ident",)], writes=[pk],
                       tag="htr")
                    if k % 8 == 7 or k == KI - 1:
                        lo8 = (k // 8) * 8
                        n = k - lo8 + 1
                        act(lambda e, lo8=lo8, n=n, pt=pt, ti=ti: e.copy(
                            hqT[:, ti, lo8:lo8 + n, :].rearrange(
                                "p a b -> p (a b)"), pt[:, 0:n * 128]),
                            reads=[pk, ("hqT", ti)], writes=[("hqT", ti)],
                            tag="htrc")
            for k in range(KI):
                for hi in range(2):
                    pe(lambda e, k=k, hi=hi: e.matmul(
                        pF[:, hi * 512:(hi + 1) * 512], hqT[:, 0, k, :],
                        s1[:, k, hi * 512:(hi + 1) * 512],
                        start=(k == 0), stop=(k == KI - 1)),
                       reads=[("hqT", 0), ("s1",)], writes=[("pF",)],
                       tag="dnmm")
                for hi in range(2):
                    pdb, pkb = (pG[0], kG[0]) if hi == 0 else (pU[0], kU[0])
                    pe(lambda e, k=k, hi=hi, pdb=pdb: e.matmul(
                        pdb[:, 0:512], hqT[:, 1, k, :],
                        s1[:, k, hi * 512:(hi + 1) * 512],
                        start=(k == 0), stop=(k == KI - 1)),
                       reads=[("hqT", 1), ("s1",)], writes=[pkb],
                       tag="dnmm")
            dve(lambda e, ta=ta: e.tensor_tensor(
                r_sb[:, ta, :], pF[:, :], r_sb[:, ta, :], op=OP.add),
                reads=[("pF",), ("r", ta)], writes=[("r", ta)], tag="resid")
            dve(lambda e, tb=tb: e.tensor_tensor(
                r_sb[:, tb, 0:512], pG[0][:, 0:512], r_sb[:, tb, 0:512],
                op=OP.add),
                reads=[kG[0], ("r", tb)], writes=[("r", tb)], tag="resid")
            dve(lambda e, tb=tb: e.tensor_tensor(
                r_sb[:, tb, 512:1024], pU[0][:, 0:512],
                r_sb[:, tb, 512:1024], op=OP.add),
                reads=[kU[0], ("r", tb)], writes=[("r", tb)], tag="resid")
            if is_last:
                off = grp * NQ * 128
                for t in (ta, tb):
                    oid = pool_dma(lambda e, t=t, off=off: e.dma_start(
                        out=out_d[off + t * 128: off + (t + 1) * 128, :],
                        in_=r_sb[:, t, :]),
                        reads=[("r", t)], writes=[("hbm_out", grp, t)],
                        tag="outst", sem="oq%d" % t)
                    out_store_ids.append(oid)

    def attn_call(grp, dbg_dd=None):
        ln_stats()
        for k in range(KD):
            sp_dma(lambda e, k=k: e.dma_start(
                out=wst[:, 0:C], in_=w_in_d[k * 128:(k + 1) * 128, :]),
                reads=[("wst",)], writes=[("wst",)], tag="awload",
                sem="smq")
            lo = k * C
            dve(lambda e, k=k: e.tensor_copy(win_b[:, k, :], wst[:, 0:C]),
                reads=[("wst",)], writes=s2k(lo, lo + C), tag="awcast")
            sp_dma(lambda e, k=k: e.dma_start(
                out=wst[:, 0:D], in_=w_out_d[k * 128:(k + 1) * 128, :]),
                reads=[("wst",)], writes=[("wst",)], tag="awload",
                sem="smq")
            dve(lambda e, k=k: e.tensor_copy(wout_b[:, k, :], wst[:, 0:D]),
                reads=[("wst",)], writes=[("wout", k), ("wrow",)], tag="awcast")
        for t in range(NQ):
            dve(lambda e, t=t: e.tensor_scalar(
                v_b[:, 0, :], r_sb[:, t, :], mcol[:, t:t + 1], rstd[:, t:t + 1],
                op0=OP.subtract, op1=OP.mult),
                reads=[("r", t), ("mcol",), ("rstd",), ("v", 0)],
                writes=[("v", 0)], tag="lnapply")
            for k in range(KD):
                pe(lambda e, k=k: e.transpose(
                    pT[:, k * 128:(k + 1) * 128],
                    v_b[:, 0, k * 128:(k + 1) * 128], ident[:, :]),
                   reads=[("v", 0), ("ident",)], writes=[("pT",)], tag="vtr")
            act(lambda e: e.copy(hqT[:, 0, 0:8, :].rearrange("p a b -> p (a b)"),
                                 pT[:, :]),
                reads=[("pT",), ("hqT", 0)], writes=[("hqT", 0)], tag="vtrc")
            for j in range(9):
                if j < 4:
                    reg, pk = pG[0][:, (j % 4) * 128:(j % 4 + 1) * 128], kG[0]
                elif j < 8:
                    reg, pk = pU[0][:, (j - 4) * 128:(j - 3) * 128], kU[0]
                else:
                    reg, pk = pF[0:SD, 0:128], ("pF",)
                for k in range(KD):
                    lh_ap = (win_b[:, k, j * 128:(j + 1) * 128] if j < 8
                             else win_b[:, k, D:C])
                    lo = k * C + (j * 128 if j < 8 else D)
                    pe(lambda e, reg=reg, lh_ap=lh_ap, k=k: e.matmul(
                        reg, lh_ap, hqT[:, 0, k, :], start=(k == 0), stop=False),
                       reads=s2k(lo, lo + (128 if j < 8 else SD)) + [("hqT", 0)],
                       writes=[pk], tag="cmbmm")
                b_ap = bin_b[:, j * 128:(j + 1) * 128] if j < 8 else bin_b[:, D:C]
                pe(lambda e, reg=reg, b_ap=b_ap: e.matmul(
                    reg, b_ap, ones_b[:, :], start=False, stop=True),
                   reads=[("bin_b",), ("ones",)], writes=[pk], tag="cmbbias")
            for j in range(8):
                src = (pG[0][:, j * 128:(j + 1) * 128] if j < 4
                       else pU[0][:, (j - 4) * 128:(j - 3) * 128])
                pk = kG[0] if j < 4 else kU[0]
                lo = AW_CMB + j * 128
                act(lambda e, j=j, src=src: e.copy(cmbT[:, j, :], src),
                    reads=[pk] + s2k(lo, lo + 128), writes=s2k(lo, lo + 128),
                    tag="cmbev")
            act(lambda e: e.copy(cmb8_b[:, :], pF[0:SD, 0:128]),
                reads=[("pF",), ("cmb8_b",)], writes=[("cmb8_b",)], tag="cmb8ev")
            dve(lambda e: e.tensor_copy(si_sb[:, :], pF[0:SD, 0:128]),
                reads=[("pF",), ("si_sb",)], writes=[("si_sb",)], tag="siev")
            if dbg_dd is not None and grp == 0 and t == 0:
                for j in range(8):
                    out_store_ids.append(pool_dma(
                        lambda e, j=j: e.dma_start(
                            out=dbg_dd["dbg_cmb"][:, j * 128:(j + 1) * 128],
                            in_=cmbT[:, j, :]),
                        reads=s2k(AW_CMB + j * 128, AW_CMB + (j + 1) * 128),
                        tag="dbgcmb"))
                out_store_ids.append(pool_dma(
                    lambda e: e.dma_start(
                        out=dbg_dd["dbg_cmb"][0:SD, 8 * 128:9 * 128],
                        in_=cmb8_b[:, :]),
                    reads=[("cmb8_b",)], tag="dbgcmb8"))
            greg = pF[0:SD, 128:256]
            for j in range(8):
                lo = AW_CMB + j * 128
                pe(lambda e, j=j, greg=greg: e.matmul(
                    greg, wg_b[:, j, :], cmbT[:, j, :],
                    start=(j == 0), stop=False),
                   reads=[("wg_b",)] + s2k(lo, lo + 128), writes=[("pF",)],
                   tag="gmm")
            pe(lambda e, greg=greg: e.matmul(
                greg, wg8_b[:, :], cmb8_b[:, :], start=False, stop=False),
               reads=[("wg8_b",), ("cmb8_b",)], writes=[("pF",)], tag="gmm8")
            pe(lambda e, greg=greg: e.matmul(
                greg, bg_b[:, :], ones_b[:, :], start=False, stop=True),
               reads=[("bg_b",), ("ones",)], writes=[("pF",)], tag="gbias")
            act(lambda e, greg=greg: e.activation(
                out=g_sb[:, :], in_=greg, func=AF.Sigmoid),
                reads=[("pF",), ("g_sb",)], writes=[("g_sb",)], tag="sigm")
            dve(lambda e: e.tensor_scalar(states[:, :], g_sb[:, :], -1.0, 1.0,
                                          op0=OP.mult, op1=OP.add),
                reads=[("g_sb",)], writes=[("states",)])
            dve(lambda e: e.tensor_tensor(si_sb[:, :], states[:, :],
                                          si_sb[:, :], op=OP.mult),
                reads=[("states",), ("si_sb",)], writes=[("si_sb",)])
            dve(lambda e: e.tensor_tensor_scan(
                states[:, :], g_sb[:, :], si_sb[:, :], carry[:, :],
                op0=OP.mult, op1=OP.add),
                reads=[("g_sb",), ("si_sb",), ("carry",)],
                writes=[("states",)], tag="scan")
            dve(lambda e: e.tensor_copy(carry[:, :], states[:, 127:128]),
                reads=[("states",)], writes=[("carry",)])
            dve(lambda e: e.tensor_copy(states_b[:, :], states[:, :]),
                reads=[("states",)], writes=[("states_b",)])
            if dbg_dd is not None:
                out_store_ids.append(pool_dma(
                    lambda e, t=t, grp=grp: e.dma_start(
                        out=dbg_dd["dbg_st"][:, (grp * NQ + t) * 128:
                                             (grp * NQ + t + 1) * 128],
                        in_=states[:, :]),
                    reads=[("states",)], tag="dbgst"))
            for a in (0, 512):
                for j in range(8):
                    pe(lambda e, j=j, a=a: e.matmul(
                        pF[:, a:a + 512], cmbT[:, j, :],
                        wout_b[:, j, a:a + 512], start=(j == 0), stop=False),
                       reads=s2k(AW_CMB + j * 128, AW_CMB + (j + 1) * 128)
                       + [("wout", j)], writes=[("pF",)], tag="omm")
                pe(lambda e, a=a: e.matmul(
                    pF[:, a:a + 512], states_b[:, :], w8o_b[:, a:a + 512],
                    start=False, stop=False),
                   reads=[("states_b",), ("w8o_b",)], writes=[("pF",)],
                   tag="omm8")
                pe(lambda e, a=a: e.matmul(
                    pF[:, a:a + 512], ones_b[:, :], bout_b[:, a:a + 512],
                    start=False, stop=True),
                   reads=[("ones",), ("bout_b",)], writes=[("pF",)],
                   tag="obias")
            dve(lambda e, t=t: e.tensor_tensor(
                r_sb[:, t, :], pF[:, :], r_sb[:, t, :], op=OP.add),
                reads=[("pF",), ("r", t)], writes=[("r", t)], tag="aresid")

    for grp in range(NGRP):
        for t in range(NQ):
            off = grp * NQ * 128
            sp_dma(lambda e, t=t, off=off: e.dma_start(
                out=r_sb[:, t, :],
                in_=x_d[off + t * 128: off + (t + 1) * 128, :]),
                reads=[("r", t)], writes=[("r", t)], tag="xload",
                sem="xq%d" % t)
        ffn_call(grp, is_last=False)
        if dbg:
            for t in range(NQ):
                off = grp * NQ * 128
                out_store_ids.append(pool_dma(
                    lambda e, t=t, off=off: e.dma_start(
                        out=dbg_d["dbg_r1"][off + t * 128: off + (t + 1) * 128, :],
                        in_=r_sb[:, t, :]),
                    reads=[("r", t)], tag="dbgr1"))
        attn_call(grp, dbg_d if dbg else None)
        if dbg:
            for t in range(NQ):
                off = grp * NQ * 128
                out_store_ids.append(pool_dma(
                    lambda e, t=t, off=off: e.dma_start(
                        out=dbg_d["dbg_r2"][off + t * 128: off + (t + 1) * 128, :],
                        in_=r_sb[:, t, :]),
                    reads=[("r", t)], tag="dbgr2"))
        ffn_call(grp, is_last=True)

    P.plan(force_need=out_store_ids)
    sem_names = sorted({op["sem"] for op in P.ops if op["need"]})
    sems = {nm: es.enter_context(nc.semaphore(nm)) for nm in sem_names}
    with nc.Block() as block:
        @block.sync
        def _(eng):
            P.emit_stream(eng, "sp", sems)

        @block.vector
        def _(eng):
            P.emit_stream(eng, "dve", sems)

        @block.scalar
        def _(eng):
            P.emit_stream(eng, "act", sems)

        @block.tensor
        def _(eng):
            P.emit_stream(eng, "pe", sems)

        @block.gpsimd
        def _(eng):
            P.emit_stream(eng, "pool", sems)
            for qn, tot in P.sem_totals.items():
                if qn in sems and tot and P.ops and any(
                        o["sem"] == qn and o["dma"] for o in P.ops):
                    eng.wait_ge(sems[qn], tot * 16)
    es.close()
    return nc


_NC_CACHE = None


def kernel(L0_attn_x, L0_attn_W_in, L0_attn_b_in, L0_attn_W_g, L0_attn_b_g,
           L0_attn_W_out, L0_attn_b_out, L0_ffn_w_gate, L0_ffn_w_up,
           L0_ffn_w_down):
    global _NC_CACHE
    if _NC_CACHE is None:
        _NC_CACHE = build_nc()
    nc = _NC_CACHE
    f32 = lambda a: np.ascontiguousarray(np.asarray(a, dtype=np.float32))
    shared = {
        "w_in": f32(L0_attn_W_in),
        "b_in": f32(L0_attn_b_in).reshape(1, C),
        "w_g": f32(L0_attn_W_g),
        "b_g": f32(L0_attn_b_g).reshape(1, SD),
        "w_out": f32(L0_attn_W_out),
        "b_out": f32(L0_attn_b_out).reshape(1, D),
        "w_gate": f32(L0_ffn_w_gate),
        "w_up": f32(L0_ffn_w_up),
        "w_down": f32(L0_ffn_w_down),
    }
    in_maps = []
    for c in range(8):
        m = dict(shared)
        m["x"] = f32(L0_attn_x[c])
        in_maps.append(m)
    res = run_bass_kernel_spmd(nc, in_maps, core_ids=list(range(8)))
    return np.stack([res.results[c]["out"] for c in range(8)], axis=0)
